# version 03 (final)
# baseline (speedup 1.0000x reference)
import numpy as np
import ml_dtypes
from contextlib import ExitStack

import concourse.bass as bass
import concourse.mybir as mybir
from concourse.bass import IndirectOffsetOnAxis
from concourse.bass_utils import run_bass_kernel_spmd

F32 = mybir.dt.float32
BF16 = mybir.dt.bfloat16
I32 = mybir.dt.int32
U32 = mybir.dt.uint32
ALU = mybir.AluOpType
AF = mybir.ActivationFunctionType
AX = mybir.AxisListType

NCORES = 8
T = 2048
D = 1024
NTT = 16
NCI = 8
CAP = 320
NE = 8
ENGINES = ("sync", "scalar", "vector", "tensor", "gpsimd")
RSQ128 = 0.08838834764831845

DEBUG_GLA = False
DEBUG_MOE = False
MOE_SERIAL = True

LAUNCH_GROUPS = [["moe"], ["attn"], ["moe2"], ["glu"], ["gla"], ["glu2"]]


class Plan:
    def __init__(self):
        self.q = {e: [] for e in ENGINES}

    def on(self, eng, fn):
        self.q[eng].append(fn)


class Ctx:
    pass


def sinc(C, name, k=1):
    C.semn[name] += k
    return C.semn[name]


def wt(C, eng, sem, val):
    if val is None or val <= 0:
        return
    h = C.semh[sem]
    C.plan.on(eng, lambda e, h=h, v=val: e.wait_ge(h, v))


def op(C, eng, fn, inc=None):
    if inc is None:
        C.plan.on(eng, lambda e, fn=fn: fn(e))
        return None
    t = sinc(C, inc)
    h = C.semh[inc]
    C.plan.on(eng, lambda e, fn=fn, h=h: fn(e).then_inc(h, 1))
    return t


QSEM = {"sync": "LD", "gpsimd": "GD", "scalar": "SD"}


def dma(C, q, out, in_, **kw):
    sem = QSEM[q]
    t = sinc(C, sem, 16)
    h = C.semh[sem]
    C.plan.on(q, lambda e, out=out, in_=in_, kw=kw, h=h: e.dma_start(out=out, in_=in_, **kw).then_inc(h, 16))
    return t


def barrier(C):
    t = sinc(C, "BAR", len(ENGINES))
    for eng in ENGINES:
        def f(e, t=t, h=C.semh["BAR"], eng=eng):
            if eng != "sync":
                e.drain()
            e.sem_inc(h, 1)
            e.wait_ge(h, t)
        C.plan.on(eng, f)


def selfwait(C, eng, sem):
    wt(C, eng, sem, C.semn[sem])



def emit_ln(C):
    tl = C.tl
    X, XNAT, XNT = tl["X"], tl["XNAT"], tl["XNT"]
    STATS, MV, LNV, RSTD = tl["STATS"], tl["MV"], tl["LNV"], tl["RSTD"]
    for tt in range(NTT):
        op(C, "vector", lambda v, tt=tt: v.bn_stats(STATS[:, tt * 12:tt * 12 + 6], X[:, tt * 1024:tt * 1024 + 512]))
        op(C, "vector", lambda v, tt=tt: v.bn_stats(STATS[:, tt * 12 + 6:tt * 12 + 12],
                                                    X[:, tt * 1024 + 512:tt * 1024 + 1024]),
           inc="VS" if tt == NTT - 1 else None)
    selfwait(C, "vector", "VS")
    for tt in range(NTT):
        op(C, "vector", lambda v, tt=tt: v.bn_aggr(MV[:, tt * 2:tt * 2 + 2], STATS[:, tt * 12:tt * 12 + 12]),
           inc="VS" if tt == NTT - 1 else None)
    vstat = C.semn["VS"]
    wt(C, "scalar", "VS", vstat)
    for tt in range(NTT):
        op(C, "scalar", lambda s, tt=tt: s.activation(LNV[:, tt:tt + 1], MV[:, tt * 2 + 1:tt * 2 + 2], AF.Ln, bias=C.tl["FPOOL"][:, 4440:4441]))
    for tt in range(NTT):
        op(C, "scalar", lambda s, tt=tt: s.activation(RSTD[:, tt:tt + 1], LNV[:, tt:tt + 1], AF.Exp, scale=-0.5),
           inc="SS" if tt == NTT - 1 else None)
    srstd = C.semn["SS"]
    wt(C, "vector", "SS", srstd)
    norm_done = []
    for tt in range(NTT):
        t = op(C, "vector", lambda v, tt=tt: v.tensor_scalar(
            XNAT[:, tt * 1024:(tt + 1) * 1024], X[:, tt * 1024:(tt + 1) * 1024],
            MV[:, tt * 2:tt * 2 + 1], RSTD[:, tt:tt + 1], ALU.subtract, ALU.mult), inc="VS")
        norm_done.append(t)
    PSB = [C.psb[6], C.psb[7]]
    IDB = tl["IDENT_BF"]
    evac_done = {}
    idx = 0
    for tt in range(NTT):
        for ci in range(NCI):
            slot = idx % 2
            wt(C, "tensor", "VS", norm_done[tt])
            if idx >= 2:
                wt(C, "tensor", "VS", evac_done[idx - 2])
            tmm = op(C, "tensor", lambda t, tt=tt, ci=ci, slot=slot: t.transpose(
                PSB[slot][:, 0:128], XNAT[:, tt * 1024 + ci * 128: tt * 1024 + ci * 128 + 128], IDB), inc="TS")
            wt(C, "vector", "TS", tmm)
            evac_done[idx] = op(C, "vector", lambda v, tt=tt, ci=ci, slot=slot: v.tensor_copy(
                XNT[:, ci * 2048 + tt * 128: ci * 2048 + tt * 128 + 128], PSB[slot][:, 0:128]), inc="VS")
            idx += 1



def emit_moe(C, first):
    global DEBUG_MOE, MOE_SERIAL
    tl, dr, PS = C.tl, C.dr, C.ps
    emit_ln(C)
    barrier(C)
    X, XNAT, XNT = tl["X"], tl["XNAT"], tl["XNT"]
    PB, PF = tl["POOL"], tl["FPOOL"]
    BUFT = [PB[:, 0:2560], PB[:, 2560:5120]]
    HT = PB[:, 5120:15360]
    W1M = [PB[:, 15360:16384], PB[:, 16384:17408]]
    W2M = [PB[:, 17408:18432], PB[:, 18432:19456]]
    G1 = [PB[:, 19456:20480], PB[:, 20480:21504]]
    G2 = [PB[:, 21504:22528], PB[:, 22528:23552]]
    YE = PB[:, 23552:26624]
    OH1 = PB[:, 26624:26752]
    OH2 = PB[:, 26752:26880]
    OHS = PB[:, 26880:27008]
    WRB = PB[:, 27008:27072]
    ZT = PB[:, 27072:28096]
    LSB8 = tl["FPOOL"][0:8, 0:2048]
    O = 2048
    EXPL, PRL, PR, V8 = PF[:, O:O+128], PF[:, O+128:O+256], PF[:, O+256:O+384], PF[:, O+384:O+512]
    E1F, E2F = PF[:, O+512:O+528], PF[:, O+528:O+544]
    POS1, POS2 = PF[:, O+544:O+560], PF[:, O+560:O+576]
    K1, K2 = PF[:, O+576:O+592], PF[:, O+592:O+608]
    SB1, SB2 = PF[:, O+608:O+624], PF[:, O+624:O+640]
    DM1, DM2 = PF[:, O+640:O+656], PF[:, O+656:O+672]
    TA1, TA2 = PF[:, O+672:O+688], PF[:, O+688:O+704]
    PW1, PW2 = PF[:, O+704:O+720], PF[:, O+720:O+736]
    SL1, SL2 = PF[:, O+736:O+752], PF[:, O+752:O+768]
    SUML, RECL = PF[:, O+768:O+784], PF[:, O+784:O+800]
    TMP8A, TMP8B = PF[:, O+800:O+928], PF[:, O+928:O+1056]
    TOTSB = PF[:, O+1056:O+1064]
    B2BC = PF[:, O+1088:O+2112]
    BCOL1 = PF[:, O+2112:O+2368]
    IOTA8 = PF[:, O+2368:O+2376]
    TOKIO = PF[:, O+2376:O+2392]
    BRCOL = tl["COL8"]
    B1ST = tl["B1ST"]
    SLOTU = tl["SLOTU"]
    I8U = tl["I8U"]
    IDENT_F = tl["IDENT_F"]

    if first:
        op(C, "gpsimd", lambda g: g.iota(IOTA8, pattern=[[1, 8]], base=0, channel_multiplier=0,
                                         allow_small_or_imprecise_dtypes=True))
        op(C, "gpsimd", lambda g: g.iota(TOKIO, pattern=[[128, 16]], base=0, channel_multiplier=1,
                                         allow_small_or_imprecise_dtypes=True), inc="GS")
        dma(C, "gpsimd", PB[:, 27008:27072].rearrange("p (c e) -> p c e", c=8),
            dr["L0_ffn_Wr"].rearrange("(c p) e -> p c e", p=128))
        dma(C, "sync", BRCOL, dr["L0_ffn_br"].rearrange("(e a) -> e a", a=1))
        b1ev = {}
        b1tm = {}
        for e in range(NE):
            if e >= 1:
                wt(C, "sync", "TS", b1tm[e - 1])
            t = dma(C, "sync", B1ST, dr["L0_ffn_b1"][e].rearrange("(m p) -> m p", p=128))
            wt(C, "tensor", "LD", t)
            if e >= 1:
                wt(C, "tensor", "VS", b1ev[e - 1])
            b1tm[e] = op(C, "tensor", lambda tn, e=e: tn.transpose(PS[1][:, 0:32], B1ST, IDENT_F), inc="TS")
            wt(C, "vector", "TS", b1tm[e])
            b1ev[e] = op(C, "vector", lambda v, e=e: v.tensor_copy(BCOL1[:, e * 32:(e + 1) * 32], PS[1][:, 0:32]),
                         inc="VS")
        zm = op(C, "vector", lambda v: v.memset(ZT, 0.0), inc="VS")
        for b in range(2):
            op(C, "vector", lambda v, b=b: v.memset(G1[b], 0.0))
            op(C, "vector", lambda v, b=b: v.memset(G2[b], 0.0), inc="VS")
        wt(C, "sync", "VS", zm)
        for r in range(20):
            dma(C, "sync", dr["moe_buf"][r * 128:(r + 1) * 128, :], ZT)
        barrier(C)

    ev_log = None
    for nch in range(4):
        if ev_log is not None:
            wt(C, "tensor", "VS", ev_log)
        def mmgrp(t, nch=nch):
            for ci in range(NCI):
                ins = t.matmul(PS[0][0:8, :], WRB[:, ci * 8:(ci + 1) * 8],
                               XNT[:, ci * 2048 + nch * 512: ci * 2048 + (nch + 1) * 512],
                               start=(ci == 0), stop=(ci == NCI - 1))
            return ins
        tm = op(C, "tensor", mmgrp, inc="TS")
        wt(C, "vector", "TS", tm)
        ev_log = op(C, "vector", lambda v, nch=nch: v.tensor_scalar(
            LSB8[:, nch * 512:(nch + 1) * 512], PS[0][0:8, :], BRCOL, None, ALU.add, ALU.bypass), inc="VS")
    vlog = C.semn["VS"]
    trev = {}
    for tt in range(NTT):
        slot = tt % 2
        if tt < 2:
            wt(C, "tensor", "VS", vlog)
        else:
            wt(C, "tensor", "VS", trev[tt - 2])
        tm = op(C, "tensor", lambda t, tt=tt, slot=slot: t.transpose(
            PS[2 + slot][:, 0:8], LSB8[0:8, tt * 128:(tt + 1) * 128], IDENT_F[0:8, 0:8]), inc="TS")
        wt(C, "vector", "TS", tm)
        trev[tt] = op(C, "vector", lambda v, tt=tt, slot=slot: v.tensor_copy(
            PRL[:, tt * 8:(tt + 1) * 8], PS[2 + slot][:, 0:8]), inc="VS")
    vtr = C.semn["VS"]
    wt(C, "scalar", "VS", vtr)
    for tt in range(NTT):
        op(C, "scalar", lambda s, tt=tt: s.activation(EXPL[:, tt * 8:(tt + 1) * 8], PRL[:, tt * 8:(tt + 1) * 8], AF.Exp),
           inc="SS" if tt == NTT - 1 else None)
    wt(C, "vector", "SS", C.semn["SS"])
    for tt in range(NTT):
        op(C, "vector", lambda v, tt=tt: v.reduce_sum(SUML[:, tt:tt + 1], EXPL[:, tt * 8:(tt + 1) * 8], axis=AX.X),
           inc="VS" if tt == NTT - 1 else None)
    selfwait(C, "vector", "VS")
    for tt in range(NTT):
        op(C, "vector", lambda v, tt=tt: v.reciprocal(RECL[:, tt:tt + 1], SUML[:, tt:tt + 1]),
           inc="VS" if tt == NTT - 1 else None)
    selfwait(C, "vector", "VS")
    for tt in range(NTT):
        op(C, "vector", lambda v, tt=tt: v.tensor_scalar(
            PR[:, tt * 8:(tt + 1) * 8], EXPL[:, tt * 8:(tt + 1) * 8], RECL[:, tt:tt + 1], None, ALU.mult, ALU.bypass),
           inc="VS" if tt == NTT - 1 else None)
    selfwait(C, "vector", "VS")
    for tt in range(NTT):
        op(C, "vector", lambda v, tt=tt: v.max(V8[:, tt * 8:(tt + 1) * 8], PR[:, tt * 8:(tt + 1) * 8]),
           inc="VS" if tt == NTT - 1 else None)
    selfwait(C, "vector", "VS")
    for tt in range(NTT):
        op(C, "vector", lambda v, tt=tt: v.max_index(I8U[:, tt * 8:(tt + 1) * 8], V8[:, tt * 8:(tt + 1) * 8],
                                                     PR[:, tt * 8:(tt + 1) * 8]),
           inc="VS" if tt == NTT - 1 else None)
    selfwait(C, "vector", "VS")
    for tt in range(NTT):
        op(C, "vector", lambda v, tt=tt: v.tensor_copy(E1F[:, tt:tt + 1], I8U[:, tt * 8:tt * 8 + 1]))
        op(C, "vector", lambda v, tt=tt: v.tensor_copy(E2F[:, tt:tt + 1], I8U[:, tt * 8 + 1:tt * 8 + 2]),
           inc="VS" if tt == NTT - 1 else None)
    selfwait(C, "vector", "VS")
    if first:
        wt(C, "vector", "GS", C.semn["GS"])
        wt(C, "vector", "GD", C.semn["GD"])
    for tt in range(NTT):
        op(C, "vector", lambda v, tt=tt: v.tensor_scalar(
            OH1[:, tt * 8:(tt + 1) * 8], IOTA8, E1F[:, tt:tt + 1], None, ALU.is_equal, ALU.bypass))
    for tt in range(NTT):
        op(C, "vector", lambda v, tt=tt: v.tensor_scalar(
            OH2[:, tt * 8:(tt + 1) * 8], IOTA8, E2F[:, tt:tt + 1], None, ALU.is_equal, ALU.bypass))
    for tt in range(NTT):
        op(C, "vector", lambda v, tt=tt: v.tensor_tensor(
            OHS[:, tt * 8:(tt + 1) * 8], OH1[:, tt * 8:(tt + 1) * 8], OH2[:, tt * 8:(tt + 1) * 8], ALU.add))
    op(C, "vector", lambda v: v.memset(TOTSB, 0.0), inc="VS")
    vohs = C.semn["VS"]
    wt(C, "tensor", "VS", vohs)
    for tt in range(NTT):
        def smm(t, tt=tt):
            t.matmul(PS[2][:, 0:8], tl["HEADONES_F"][0:8, 0:128], TOTSB[0:8, :], start=True, stop=False)
            return t.matmul(PS[2][:, 0:8], tl["TRI_BF"], OHS[:, tt * 8:(tt + 1) * 8], start=False, stop=True)
        tsm = op(C, "tensor", smm, inc="TS")
        tcm = op(C, "tensor", lambda t, tt=tt: t.matmul(
            PS[3][0:1, 0:8], tl["ONESCOL_BF"], OHS[:, tt * 8:(tt + 1) * 8],
            start=(tt == 0), stop=(tt == NTT - 1)), inc="TS")
        wt(C, "vector", "TS", tcm)
        op(C, "vector", lambda v: v.tensor_copy(TOTSB[0:1, :], PS[3][0:1, 0:8]))
        op(C, "vector", lambda v, tt=tt: v.tensor_tensor(
            TMP8A[:, tt * 8:(tt + 1) * 8], PS[2][:, 0:8], OH1[:, tt * 8:(tt + 1) * 8], ALU.mult))
        vm = op(C, "vector", lambda v, tt=tt: v.tensor_tensor(
            TMP8B[:, tt * 8:(tt + 1) * 8], PS[2][:, 0:8], OH2[:, tt * 8:(tt + 1) * 8], ALU.mult), inc="VS")
        wt(C, "tensor", "VS", vm)
    selfwait(C, "vector", "VS")
    for tt in range(NTT):
        op(C, "vector", lambda v, tt=tt: v.reduce_sum(POS1[:, tt:tt + 1], TMP8A[:, tt * 8:(tt + 1) * 8], axis=AX.X))
        op(C, "vector", lambda v, tt=tt: v.reduce_sum(POS2[:, tt:tt + 1], TMP8B[:, tt * 8:(tt + 1) * 8], axis=AX.X),
           inc="VS" if tt == NTT - 1 else None)
    selfwait(C, "vector", "VS")
    for POS, K, SB, DM, TA, SL, E_F, PW, vcol in (
            (POS1, K1, SB1, DM1, TA1, SL1, E1F, PW1, 0),
            (POS2, K2, SB2, DM2, TA2, SL2, E2F, PW2, 1)):
        for tt in range(NTT):
            op(C, "vector", lambda v, tt=tt, POS=POS, K=K: v.tensor_scalar(
                K[:, tt:tt + 1], POS[:, tt:tt + 1], 320.0, None, ALU.is_lt, ALU.bypass))
        for tt in range(NTT):
            op(C, "vector", lambda v, tt=tt, POS=POS, DM=DM: v.tensor_scalar(
                DM[:, tt:tt + 1], POS[:, tt:tt + 1], 320.0, None, ALU.is_ge, ALU.bypass))
        for tt in range(NTT):
            op(C, "vector", lambda v, tt=tt, POS=POS, SB=SB, E_F=E_F: v.scalar_tensor_tensor(
                SB[:, tt:tt + 1], E_F[:, tt:tt + 1], 320.0, POS[:, tt:tt + 1], ALU.mult, ALU.add))
        for tt in range(NTT):
            op(C, "vector", lambda v, tt=tt, SB=SB, TA=TA: v.scalar_tensor_tensor(
                TA[:, tt:tt + 1], SB[:, tt:tt + 1], -1.0, TOKIO[:, tt:tt + 1], ALU.mult, ALU.add))
        for tt in range(NTT):
            op(C, "vector", lambda v, tt=tt, TA=TA: v.tensor_scalar(
                TA[:, tt:tt + 1], TA[:, tt:tt + 1], 4000.0, None, ALU.add, ALU.bypass))
        for tt in range(NTT):
            op(C, "vector", lambda v, tt=tt, TA=TA, DM=DM: v.tensor_tensor(
                TA[:, tt:tt + 1], TA[:, tt:tt + 1], DM[:, tt:tt + 1], ALU.mult))
        for tt in range(NTT):
            op(C, "vector", lambda v, tt=tt, SB=SB, TA=TA, SL=SL: v.tensor_tensor(
                SL[:, tt:tt + 1], SB[:, tt:tt + 1], TA[:, tt:tt + 1], ALU.add))
        for tt in range(NTT):
            op(C, "vector", lambda v, tt=tt, K=K, PW=PW, vcol=vcol: v.scalar_tensor_tensor(
                PW[:, tt:tt + 1], V8[:, tt * 8 + vcol:tt * 8 + vcol + 1], 0.5, K[:, tt:tt + 1], ALU.mult, ALU.mult))
        for tt in range(NTT):
            op(C, "vector", lambda v, tt=tt, SL=SL, vcol=vcol: v.tensor_copy(
                SLOTU[:, vcol * 16 + tt:vcol * 16 + tt + 1], SL[:, tt:tt + 1]),
               inc="VS" if tt == NTT - 1 else None)
    vslots = C.semn["VS"]
    wt(C, "gpsimd", "VS", vslots)
    if first:
        wt(C, "gpsimd", "LD", C.semn["LD"])
    hgd = C.semh["GD"]
    for tt in range(NTT):
        for k in range(2):
            sinc(C, "GD", 16)
            def fsc(g, tt=tt, k=k, hgd=hgd):
                g.indirect_dma_start(
                    out=dr["moe_buf"], out_offset=IndirectOffsetOnAxis(ap=SLOTU[:, k * 16 + tt:k * 16 + tt + 1], axis=0),
                    in_=XNAT[:, tt * 1024:(tt + 1) * 1024], in_offset=None,
                    bounds_check=2559, oob_is_err=False).then_inc(hgd, 16)
                g.free_register(C.rfree)
            C.plan.on("gpsimd", fsc)
    scat_done = C.semn["GD"]
    buft_free = [None, None]
    w1_free = [None, None]
    w2_free = [None, None]
    relu_ev = {}
    ye_evac_last = None
    ye_store_last = None
    BN = PB[:, 19456:22528]
    B2ROWF = PF[0:1, O:O + 1024]
    bn_free = None
    for e in range(NE):
        slot = e % 2
        wt(C, "sync", "GD", scat_done)
        if bn_free is not None:
            wt(C, "sync", "TS", bn_free)
        rowsets = ((0, 128), (1, 128), (2, 64))
        for t3i, rows in rowsets:
            tb = dma(C, "sync", BN[0:rows, t3i * 1024:(t3i + 1) * 1024],
                     dr["moe_buf"][e * 320 + t3i * 128: e * 320 + t3i * 128 + rows, :])
        PSBm = [C.psb[0], C.psb[1]]
        tridx = 0
        ev_tr = {}
        for ci in range(NCI):
            for t3i, rows in rowsets:
                tslot = tridx % 2
                if tridx < 2:
                    wt(C, "tensor", "LD", tb)
                    pv = relu_ev.get((e - 1, 30 + tridx))
                    if pv:
                        wt(C, "tensor", "SS", pv)
                else:
                    wt(C, "tensor", "VS", ev_tr[tridx - 2])
                ttr = op(C, "tensor", lambda t, ci=ci, t3i=t3i, rows=rows, tslot=tslot: t.transpose(
                    PSBm[tslot][0:128, 0:rows], BN[0:rows, t3i * 1024 + ci * 128: t3i * 1024 + ci * 128 + 128],
                    tl["IDENT_BF"][0:rows, 0:rows]), inc="TS")
                wt(C, "vector", "TS", ttr)
                ev_tr[tridx] = op(C, "vector", lambda v, ci=ci, t3i=t3i, rows=rows, tslot=tslot, slot=slot: v.tensor_copy(
                    BUFT[slot][:, ci * 320 + t3i * 128: ci * 320 + t3i * 128 + rows], PSBm[tslot][0:128, 0:rows]),
                    inc="VS")
                tridx += 1
        bn_free = ttr
        buft_ready = ev_tr[tridx - 1]
        b2l = dma(C, "sync", B2ROWF, dr["L0_ffn_b2"][e:e + 1, :])
        wt(C, "tensor", "LD", b2l)
        wt(C, "tensor", "VS", buft_ready)
        if ye_evac_last is not None:
            wt(C, "tensor", "VS", ye_evac_last)
        mmb = None
        for nch in range(2):
            mmb = op(C, "tensor", lambda t, nch=nch: t.matmul(
                PS[nch][:, 0:512], tl["HEADONES_F"][0:1, 0:128], B2ROWF[0:1, nch * 512:(nch + 1) * 512],
                start=True, stop=True), inc="TS")
        wt(C, "vector", "TS", mmb)
        for nch in range(2):
            b2bc_ready = op(C, "vector", lambda v, nch=nch: v.tensor_copy(
                B2BC[:, nch * 512:(nch + 1) * 512], PS[nch][:, 0:512]), inc="VS")
        if e == 0:
            wt(C, "gpsimd", "GD", scat_done)
        for m in range(32):
            ws = m % 2
            if w1_free[ws] is not None:
                wt(C, "gpsimd", "TS", w1_free[ws])
            w1l = dma(C, "gpsimd", W1M[ws].rearrange("p (c m) -> p c m", c=8),
                      dr["L0_ffn_W1"][e][:, m * 128:(m + 1) * 128].rearrange("(c p) m -> p c m", p=128))
            wt(C, "tensor", "GD", w1l)
            if m <= 1:
                wt(C, "tensor", "VS", b2bc_ready)
            prev = relu_ev.get((e, m - 2)) or relu_ev.get((e - 1, 30 + m))
            if prev:
                wt(C, "tensor", "SS", prev)
            def f2(t, m=m, ws=ws, slot=slot):
                for ci in range(NCI):
                    ins = t.matmul(PS[m % 2][:, 0:320], W1M[ws][:, ci * 128:(ci + 1) * 128],
                                   BUFT[slot][:, ci * 320:(ci + 1) * 320],
                                   start=(ci == 0), stop=(ci == NCI - 1))
                return ins
            tmm = op(C, "tensor", f2, inc="TS")
            w1_free[ws] = tmm
            if m == 31:
                buft_free[slot] = tmm
            wt(C, "scalar", "TS", tmm)
            relu_ev[(e, m)] = op(C, "scalar", lambda s, e=e, m=m: s.activation(
                HT[:, m * 320:(m + 1) * 320], PS[m % 2][:, 0:320], AF.Relu,
                bias=BCOL1[:, e * 32 + m:e * 32 + m + 1]), inc="SS")
        if DEBUG_MOE and e == 0:
            barrier(C)
            dma(C, "sync", dr["moe_hdbg"], HT)
            t_ = dma(C, "sync", dr["moe_bdbg"], BUFT[0])
            wt(C, "sync", "LD", t_)
            barrier(C)
        if MOE_SERIAL in (True, "A"):
            barrier(C)
        ydims = ((0, 128), (1, 128), (2, 64))
        for m in range(32):
            ws = m % 2
            if w2_free[ws] is not None:
                wt(C, "gpsimd", "TS", w2_free[ws])
            w2l = dma(C, "gpsimd", W2M[ws], dr["L0_ffn_W2"][e][m * 128:(m + 1) * 128, :])
            wt(C, "tensor", "GD", w2l)
            wt(C, "tensor", "SS", relu_ev[(e, m)])
            if m == 0 and ye_evac_last is not None:
                wt(C, "tensor", "VS", ye_evac_last)
            def f3(t, m=m, ws=ws):
                for s, rows in ydims:
                    for nch in range(2):
                        ins = t.matmul(PS[2 + 2 * s + nch][0:rows, :],
                                       HT[:, m * 320 + s * 128: m * 320 + s * 128 + rows],
                                       W2M[ws][:, nch * 512:(nch + 1) * 512],
                                       start=(m == 0), stop=(m == 31))
                return ins
            tmm = op(C, "tensor", f3, inc="TS")
            w2_free[ws] = tmm
        ty = tmm
        wt(C, "vector", "TS", ty)
        if ye_store_last is not None:
            wt(C, "vector", "LD", ye_store_last)
        for s, rows in ydims:
            for nch in range(2):
                ye_evac_last = op(C, "vector", lambda v, s=s, nch=nch, rows=rows: v.tensor_tensor(
                    YE[0:rows, s * 1024 + nch * 512: s * 1024 + (nch + 1) * 512],
                    PS[2 + 2 * s + nch][0:rows, :], B2BC[0:rows, nch * 512:(nch + 1) * 512], ALU.add), inc="VS")
        wt(C, "sync", "VS", ye_evac_last)
        for s, rows in ydims:
            ye_store_last = dma(C, "sync", dr["moe_y"][e * 320 + s * 128: e * 320 + s * 128 + rows, :],
                                YE[0:rows, s * 1024:(s + 1) * 1024])
        if MOE_SERIAL in (True, "B"):
            wt(C, "sync", "LD", ye_store_last)
            barrier(C)
    if DEBUG_MOE:
        wt(C, "sync", "VS", C.semn["VS"])
        dma(C, "sync", dr["moe_fdbg"], tl["FPOOL"])
        dma(C, "sync", dr["moe_sdbg"], SLOTU)
    wt(C, "gpsimd", "LD", ye_store_last)
    g_done = {}
    for tt in range(NTT):
        gs = tt % 2
        if tt >= 2:
            wt(C, "gpsimd", "VS", g_done[tt - 2])
        for k, G in ((0, G1), (1, G2)):
            sinc(C, "GD", 16)
            def fg(g, tt=tt, k=k, G=G, gs=gs, hgd=hgd):
                g.indirect_dma_start(
                    out=G[gs], out_offset=None, in_=dr["moe_y"],
                    in_offset=IndirectOffsetOnAxis(ap=SLOTU[:, k * 16 + tt:k * 16 + tt + 1], axis=0),
                    bounds_check=2559, oob_is_err=False).then_inc(hgd, 16)
                g.free_register(C.rfree)
            C.plan.on("gpsimd", fg)
        gd = C.semn["GD"]
        wt(C, "vector", "GD", gd)
        op(C, "vector", lambda v, tt=tt, gs=gs: v.scalar_tensor_tensor(
            X[:, tt * 1024:(tt + 1) * 1024], G1[gs], PW1[:, tt:tt + 1], X[:, tt * 1024:(tt + 1) * 1024],
            ALU.mult, ALU.add))
        g_done[tt] = op(C, "vector", lambda v, tt=tt, gs=gs: v.scalar_tensor_tensor(
            X[:, tt * 1024:(tt + 1) * 1024], G2[gs], PW2[:, tt:tt + 1], X[:, tt * 1024:(tt + 1) * 1024],
            ALU.mult, ALU.add), inc="VS")
    barrier(C)



def emit_attn(C):
    tl, dr, PS = C.tl, C.dr, C.ps
    emit_ln(C)
    barrier(C)
    X, XNAT, XNT = tl["X"], tl["XNAT"], tl["XNT"]
    PB, PF = tl["POOL"], tl["FPOOL"]
    Q8 = PB[:, 0:8192]
    K8 = PB[:, 8192:16384]
    VAUG = PB[:, 16384:24704]
    WQKV = [PB[:, 24704:25728], PB[:, 25728:26752]]
    WVG = PB[:, 24704:28800]
    E2 = [PB[:, 24704:25216], PB[:, 25216:25728]]
    RDB = [PB[:, 25728:26240], PB[:, 26240:26752]]
    WOUT = PB[:, 0:8192]
    CTX = XNAT
    BQCOL = PF[:, 4368:4376]
    BKCOL = PF[:, 4376:4384]
    BVBC = PF[:, 272:784]
    BOUTBC = PF[:, 784:1808]
    BROWF = PF[0:1, 1808:2832]
    BVROWF = PF[0:1, 2832:3344]
    RBS = [PF[:, 1808:2320], PF[:, 2320:2832]]
    RDEN = [PF[0:1, 2832:3344], PF[0:1, 3344:3856]]
    TMPF = PF[:, 3856:4368]
    B1ST = tl["B1ST"]
    IDENT_F = tl["IDENT_F"]
    CM01 = tl["CM01_BF"]

    wqkv = dr["L0_attn_Wqkv"]
    bqkv = dr["L0_attn_bqkv"]

    prev_ev = None
    prev_tm = None
    for i, COL in ((0, BQCOL), (1, BKCOL)):
        if prev_tm is not None:
            wt(C, "sync", "TS", prev_tm)
        t = dma(C, "sync", B1ST[0:8, :], bqkv[i * 1024:(i + 1) * 1024].rearrange("(a p) -> a p", p=128))
        wt(C, "tensor", "LD", t)
        if prev_ev is not None:
            wt(C, "tensor", "VS", prev_ev)
        prev_tm = op(C, "tensor", lambda tn, i=i: tn.transpose(PS[1][:, 0:8], B1ST[0:8, :], IDENT_F[0:8, 0:8]),
                     inc="TS")
        wt(C, "vector", "TS", prev_tm)
        prev_ev = op(C, "vector", lambda v, COL=COL: v.tensor_copy(COL, PS[1][:, 0:8]), inc="VS")
    ONESF = tl["HEADONES_F"][0:1, 0:128]
    ldrow = dma(C, "sync", BROWF, dr["L0_attn_bout"].rearrange("(a c) -> a c", a=1))
    wt(C, "tensor", "LD", ldrow)
    mmb = None
    for nch in range(2):
        mmb = op(C, "tensor", lambda t, nch=nch, ONESF=ONESF: t.matmul(
            PS[4 + nch][:, 0:512], ONESF, BROWF[0:1, nch * 512:(nch + 1) * 512],
            start=True, stop=True), inc="TS")
    wt(C, "vector", "TS", mmb)
    for nch in range(2):
        bout_ready = op(C, "vector", lambda v, nch=nch: v.tensor_copy(
            BOUTBC[:, nch * 512:(nch + 1) * 512], PS[4 + nch][:, 0:512]), inc="VS")

    wq_free = [None, None]
    es_free = [None, None]
    ctx_free = [None, None]
    rb_free = [None, None]
    e2_ready = {}
    ctx_norm_done = []
    last_ctx_mm = None
    vaug_ms = None
    for g in range(2):
        if g == 1:
            barrier(C)
        idx = 0
        evq = {}
        for kind in range(2):
            base = kind * 1024 + g * 512
            COL = BQCOL if kind == 0 else BKCOL
            DEST = Q8 if kind == 0 else K8
            for jrel in range(4):
                ws = idx % 2
                if wq_free[ws] is not None:
                    wt(C, "gpsimd", "TS", wq_free[ws])
                ld = dma(C, "gpsimd", WQKV[ws].rearrange("p (c m) -> p c m", c=8),
                         wqkv[:, base + jrel * 128: base + jrel * 128 + 128].rearrange("(c p) m -> p c m", p=128))
                for tc in range(4):
                    pslot = (idx * 4 + tc) % 2
                    wt(C, "tensor", "GD", ld)
                    pv = evq.get((idx * 4 + tc - 2))
                    if pv:
                        wt(C, "tensor", "VS", pv)
                    def qmm(t, ws=ws, tc=tc, pslot=pslot):
                        for ci in range(NCI):
                            ins = t.matmul(PS[pslot], WQKV[ws][:, ci * 128:(ci + 1) * 128],
                                           XNT[:, ci * 2048 + tc * 512: ci * 2048 + (tc + 1) * 512],
                                           start=(ci == 0), stop=(ci == NCI - 1))
                        return ins
                    tmm = op(C, "tensor", qmm, inc="TS")
                    if tc == 3:
                        wq_free[ws] = tmm
                    wt(C, "vector", "TS", tmm)
                    jglob = g * 4 + jrel
                    if kind == 0:
                        ev = op(C, "vector", lambda v, jrel=jrel, tc=tc, pslot=pslot, jglob=jglob, DEST=DEST, COL=COL:
                                v.tensor_scalar(DEST[:, jrel * 2048 + tc * 512: jrel * 2048 + (tc + 1) * 512],
                                                PS[pslot], COL[:, jglob:jglob + 1], None, ALU.add, ALU.bypass),
                                inc="VS")
                    else:
                        ev = op(C, "vector", lambda v, jrel=jrel, tc=tc, pslot=pslot, jglob=jglob, DEST=DEST, COL=COL:
                                v.tensor_scalar(DEST[:, jrel * 2048 + tc * 512: jrel * 2048 + (tc + 1) * 512],
                                                PS[pslot], COL[:, jglob:jglob + 1], 0.125, ALU.add, ALU.mult),
                                inc="VS")
                    evq[idx * 4 + tc] = ev
                idx += 1
        qk_done = C.semn["VS"]
        wt(C, "gpsimd", "TS", max(wq_free))
        ldv = None
        for ci in range(NCI):
            ldv = dma(C, "gpsimd", WVG[:, ci * 512:(ci + 1) * 512],
                      wqkv[ci * 128:(ci + 1) * 128, 2048 + g * 512: 2048 + (g + 1) * 512])
        ldbv = dma(C, "sync", BVROWF, bqkv[2048 + g * 512: 2048 + (g + 1) * 512]
                   .rearrange("(a c) -> a c", a=1))
        wt(C, "tensor", "LD", ldbv)
        mmbv = op(C, "tensor", lambda t, ONESF=ONESF: t.matmul(PS[2][:, 0:512], ONESF, BVROWF,
                                                               start=True, stop=True), inc="TS")
        wt(C, "vector", "TS", mmbv)
        bv_ready = op(C, "vector", lambda v: v.tensor_copy(BVBC, PS[2][:, 0:512]), inc="VS")
        if last_ctx_mm is not None:
            wt(C, "vector", "TS", last_ctx_mm)
        vaug_ms = op(C, "vector", lambda v: v.memset(VAUG, 1.0), inc="VS")
        ev_v = {}
        for kt in range(NTT):
            pslot = kt % 2
            wt(C, "tensor", "GD", ldv)
            wt(C, "tensor", "VS", ev_v.get(kt - 2, qk_done))
            def vmm(t, kt=kt, pslot=pslot):
                for ci in range(NCI):
                    ins = t.matmul(PS[pslot], XNT[:, ci * 2048 + kt * 128: ci * 2048 + kt * 128 + 128],
                                   WVG[:, ci * 512:(ci + 1) * 512],
                                   start=(ci == 0), stop=(ci == NCI - 1))
                return ins
            tmm = op(C, "tensor", vmm, inc="TS")
            wt(C, "vector", "TS", tmm)
            wt(C, "vector", "VS", bv_ready)
            wt(C, "vector", "VS", vaug_ms)
            for hr in range(8):
                ev = op(C, "vector", lambda v, kt=kt, hr=hr, pslot=pslot: v.tensor_tensor(
                    VAUG[:, kt * 520 + hr * 65: kt * 520 + hr * 65 + 64],
                    PS[pslot][:, hr * 64:(hr + 1) * 64], BVBC[:, hr * 64:(hr + 1) * 64], ALU.add),
                    inc="VS" if hr == 7 else None)
            ev_v[kt] = ev
        v_done = C.semn["VS"]
        for hr in range(8):
            h = g * 8 + hr
            jh = hr // 2
            bs = 64 * (hr % 2)
            ch = h // 2
            cb = 64 * (h % 2)
            for qc in range(4):
                hq = hr * 4 + qc
                cslot = hq % 2
                ktmax = 4 * qc + 4
                ctx_mm_prev = {}
                for kt in range(ktmax):
                    drel = kt - 4 * qc
                    win0 = max(drel, 0) * 128
                    es = kt % 2
                    if es_free[es] is not None:
                        wt(C, "tensor", "SS", es_free[es])
                    if kt == 0:
                        wt(C, "tensor", "VS", v_done)
                        if ctx_free[cslot] is not None:
                            wt(C, "tensor", "VS", ctx_free[cslot])
                    tsm = op(C, "tensor", lambda t, jh=jh, bs=bs, kt=kt, qc=qc, win0=win0, es=es: t.matmul(
                        PS[es][:, win0:512],
                        K8[bs:bs + 64, jh * 2048 + kt * 128: jh * 2048 + kt * 128 + 128],
                        Q8[bs:bs + 64, jh * 2048 + qc * 512 + win0: jh * 2048 + (qc + 1) * 512],
                        start=True, stop=True), inc="TS")
                    wt(C, "scalar", "TS", tsm)
                    pvctx = ctx_mm_prev.get(kt - 2)
                    if pvctx:
                        wt(C, "scalar", "TS", pvctx)
                    sse = op(C, "scalar", lambda s, es=es, win0=win0: s.activation(
                        E2[es][:, win0:512], PS[es][:, win0:512], AF.Exp, bias=C.tl["FPOOL"][:, 4441:4442]), inc="SS")
                    es_free[es] = sse
                    eready = ("SS", sse)
                    if drel >= 0:
                        wt(C, "vector", "SS", sse)
                        vd = op(C, "vector", lambda v, es=es, win0=win0: v.tensor_tensor(
                            E2[es][:, win0:win0 + 128], E2[es][:, win0:win0 + 128], CM01, ALU.mult), inc="VS")
                        eready = ("VS", vd)
                    wt(C, "tensor", eready[0], eready[1])
                    cmm = op(C, "tensor", lambda t, hr=hr, kt=kt, win0=win0, es=es, cslot=cslot, ktmax=ktmax: t.matmul(
                        PS[2 + cslot][0:65, win0:512],
                        VAUG[:, kt * 520 + hr * 65: kt * 520 + hr * 65 + 65],
                        E2[es][:, win0:512],
                        start=(kt == 0), stop=(kt == ktmax - 1)), inc="TS")
                    ctx_mm_prev[kt] = cmm
                last_ctx_mm = cmm
                wt(C, "vector", "TS", cmm)
                if rb_free[cslot] is not None:
                    wt(C, "vector", "TS", rb_free[cslot])
                op(C, "vector", lambda v, cslot=cslot: v.reciprocal(RDEN[cslot], PS[2 + cslot][64:65, :]), inc="VS")
                selfwait(C, "vector", "VS")
                rv = op(C, "vector", lambda v, cslot=cslot: v.tensor_copy(RDB[cslot][0:1, 0:512], RDEN[cslot]),
                        inc="VS")
                wt(C, "tensor", "VS", rv)
                rbm = op(C, "tensor", lambda t, cslot=cslot: t.matmul(
                    PS[4 + cslot][0:64, :], tl["ONES1x64_BF"], RDB[cslot][0:1, 0:512],
                    start=True, stop=True), inc="TS")
                rb_free[cslot] = rbm
                wt(C, "vector", "TS", rbm)
                op(C, "vector", lambda v, cslot=cslot: v.tensor_copy(RBS[cslot][0:64, :], PS[4 + cslot][0:64, :]),
                   inc="VS")
                selfwait(C, "vector", "VS")
                cf = op(C, "vector", lambda v, cslot=cslot, cb=cb, ch=ch, qc=qc: v.tensor_tensor(
                    CTX[cb:cb + 64, ch * 2048 + qc * 512: ch * 2048 + (qc + 1) * 512],
                    PS[2 + cslot][0:64, :], RBS[cslot][0:64, :], ALU.mult), inc="VS")
                ctx_free[cslot] = cf
                ctx_norm_done.append(cf)
    ctx_all = C.semn["VS"]
    ldw = None
    wt(C, "gpsimd", "TS", last_ctx_mm)
    for chh in range(NCI):
        ldw = dma(C, "gpsimd", WOUT[:, chh * 1024:(chh + 1) * 1024],
                  dr["L0_attn_Wout"][chh * 128:(chh + 1) * 128, :])
    evo = {}
    for tt in range(NTT):
        for nch in range(2):
            i = tt * 2 + nch
            pslot = i % 2
            wt(C, "tensor", "GD", ldw)
            wt(C, "tensor", "VS", evo.get(i - 2, ctx_all))
            def omm(t, tt=tt, nch=nch, pslot=pslot):
                for chh in range(NCI):
                    ins = t.matmul(PS[pslot], CTX[:, chh * 2048 + tt * 128: chh * 2048 + tt * 128 + 128],
                                   WOUT[:, chh * 1024 + nch * 512: chh * 1024 + (nch + 1) * 512],
                                   start=(chh == 0), stop=(chh == NCI - 1))
                return ins
            tmm = op(C, "tensor", omm, inc="TS")
            wt(C, "vector", "TS", tmm)
            if tt == 0 and nch == 0:
                wt(C, "vector", "VS", bout_ready)
            op(C, "vector", lambda v, pslot=pslot, nch=nch: v.tensor_tensor(
                TMPF, PS[pslot], BOUTBC[:, nch * 512:(nch + 1) * 512], ALU.add))
            evo[i] = op(C, "vector", lambda v, tt=tt, nch=nch: v.tensor_tensor(
                X[:, tt * 1024 + nch * 512: tt * 1024 + (nch + 1) * 512], TMPF,
                X[:, tt * 1024 + nch * 512: tt * 1024 + (nch + 1) * 512], ALU.add), inc="VS")
    barrier(C)



def emit_gla(C):
    global DEBUG_GLA
    tl, dr, PS = C.tl, C.dr, C.ps
    emit_ln(C)
    barrier(C)
    X, XNAT, XNT = tl["X"], tl["XNAT"], tl["XNT"]
    PB, PF = tl["POOL"], tl["FPOOL"]
    STG = [PB[:, 0:2048], PB[:, 2048:4096]]
    WW = [PB[:, 4096:5120], PB[:, 5120:6144]]
    WGB = PB[:, 6144:6208]
    WRR = PB[:, 4096:20480]
    QT = PB[:, 0:4096]
    KT = PB[:, 4096:8192]
    KN = PB[:, 8192:12288]
    VN = PB[:, 12288:20480]
    OG = PB[:, 20480:28672]
    S = XNAT[:, 10240:12288]
    OUT = XNAT[:, 0:8192]
    GT = [XNAT[:, 8192:8320], XNAT[:, 8320:8448]]
    DT = [XNAT[:, 8448:8576], XNAT[:, 8576:8704]]
    QG = [XNAT[:, 8704:8832], XNAT[:, 8832:8960]]
    A1T = [XNAT[:, 8960:9088], XNAT[:, 9088:9216]]
    AT = [XNAT[:, 9216:9344], XNAT[:, 9344:9472]]
    KD = [XNAT[:, 9472:9600], XNAT[:, 9600:9728]]
    SST = [XNAT[:, 9728:9984], XNAT[:, 9984:10240]]
    WOR = XNT[:, 0:16384]
    ALPHA = tl["FPOOL"][0:8, 0:2048]
    LT = PF[0:8, 2048:4096]
    LNAT = PF[:, 4096:4224]
    TCOL = PF[0:8, 4224:4240]
    NEGL = [PF[:, 4240:4241], PF[:, 4241:4242]]
    TBS = [PF[:, 4242:4243], PF[:, 4243:4244]]
    KDC = [PF[:, 4244:4245], PF[:, 4245:4246]]
    GAM = [PF[:, 4246:4247], PF[:, 4247:4248]]
    BGC = PF[0:8, 4248:4249]
    DTMP = PF[:, 4250:4378]
    BG8 = tl["COL8"]
    IDENT_F = tl["IDENT_F"]
    IDENT_BF = tl["IDENT_BF"]
    CM01 = tl["CM01_BF"]
    HONES = tl["HEADONES_F"]

    dma(C, "gpsimd", WGB.rearrange("p (c e) -> p c e", c=8),
        dr["L1_attn_Wg"].rearrange("(c p) e -> p c e", p=128))
    tbg = dma(C, "sync", BG8, dr["L1_attn_bg"].rearrange("(e a) -> e a", a=1))
    wt(C, "vector", "LD", tbg)
    vbg = op(C, "vector", lambda v: v.tensor_scalar(BGC, BG8, 1.0 / 16.0, None, ALU.mult, ALU.bypass), inc="VS")
    ev_al = None
    for nch in range(4):
        wt(C, "tensor", "GD", C.semn["GD"])
        if ev_al is not None:
            wt(C, "tensor", "SS", ev_al)
        def amm(t, nch=nch):
            for ci in range(NCI):
                ins = t.matmul(PS[0][0:8, :], WGB[:, ci * 8:(ci + 1) * 8],
                               XNT[:, ci * 2048 + nch * 512: ci * 2048 + (nch + 1) * 512],
                               start=(ci == 0), stop=(ci == NCI - 1))
            return ins
        tm = op(C, "tensor", amm, inc="TS")
        wt(C, "scalar", "TS", tm)
        if nch == 0:
            wt(C, "scalar", "VS", vbg)
        ev_al = op(C, "scalar", lambda s, nch=nch: s.activation(
            ALPHA[:, nch * 512:(nch + 1) * 512], PS[0][0:8, :], AF.Sigmoid, bias=BGC, scale=1.0 / 16.0), inc="SS")
    selfwait(C, "scalar", "SS")
    for nch in range(4):
        op(C, "scalar", lambda s, nch=nch: s.activation(
            ALPHA[:, nch * 512:(nch + 1) * 512], ALPHA[:, nch * 512:(nch + 1) * 512], AF.Ln),
           inc="SS" if nch == 3 else None)
    wt(C, "vector", "SS", C.semn["SS"])
    for n in range(16):
        op(C, "vector", lambda v, n=n: v.tensor_tensor_scan(
            LT[:, n * 128:(n + 1) * 128], ALPHA[0:8, n * 128:(n + 1) * 128],
            ALPHA[0:8, n * 128:(n + 1) * 128], 0.0, ALU.add, ALU.bypass),
           inc="VS" if n == 15 else None)
    selfwait(C, "vector", "VS")
    for n in range(16):
        op(C, "vector", lambda v, n=n: v.tensor_copy(TCOL[:, n:n + 1], LT[:, n * 128 + 127: n * 128 + 128]),
           inc="VS" if n == 15 else None)
    vlt = C.semn["VS"]
    wt(C, "tensor", "VS", vlt)
    ev_ln = {}
    for n in range(16):
        slot = n % 2
        if n >= 2:
            wt(C, "tensor", "VS", ev_ln[n - 2])
        tm = op(C, "tensor", lambda t, n=n, slot=slot: t.transpose(
            PS[2 + slot][:, 0:8], LT[:, n * 128:(n + 1) * 128], IDENT_F[0:8, 0:8]), inc="TS")
        wt(C, "vector", "TS", tm)
        ev_ln[n] = op(C, "vector", lambda v, n=n, slot=slot: v.tensor_copy(
            LNAT[:, n * 8:(n + 1) * 8], PS[2 + slot][:, 0:8]), inc="VS")
    wfree = [None, None]
    stg_free = [None, None]
    for kind, wsrc, ddst, scale in ((0, "L1_attn_Wq", "gla_q", None), (1, "L1_attn_Wk", "gla_k", RSQ128),
                                    (2, "L1_attn_Wog", "gla_ogT", "sig")):
        nj = 16 if kind == 2 else 8
        if kind == 0:
            evd = {}
        for h in range(nj):
            idx = kind * 8 + h
            ws = idx % 2
            ss = idx % 2
            if wfree[ws] is not None:
                wt(C, "gpsimd", "TS", wfree[ws])
            ld = dma(C, "gpsimd", WW[ws].rearrange("p (c m) -> p c m", c=8),
                     dr[wsrc][:, h * 128:(h + 1) * 128].rearrange("(c p) m -> p c m", p=128))
            if stg_free[ss] is not None:
                wt(C, "scalar" if scale == "sig" else "vector", "LD", stg_free[ss])
            evs = None
            for tc in range(4):
                pslot = tc % 2
                wt(C, "tensor", "GD", ld)
                pv = evd.get(idx * 4 + tc - 2)
                if pv:
                    wt(C, "tensor", pv[0], pv[1])
                def pmm(t, ws=ws, tc=tc, pslot=pslot):
                    for ci in range(NCI):
                        ins = t.matmul(PS[pslot], WW[ws][:, ci * 128:(ci + 1) * 128],
                                       XNT[:, ci * 2048 + tc * 512: ci * 2048 + (tc + 1) * 512],
                                       start=(ci == 0), stop=(ci == NCI - 1))
                    return ins
                tmm = op(C, "tensor", pmm, inc="TS")
                if tc == 3:
                    wfree[ws] = tmm
                if scale == "sig":
                    wt(C, "scalar", "TS", tmm)
                    evs = op(C, "scalar", lambda sc, ss=ss, tc=tc, pslot=pslot: sc.activation(
                        STG[ss][:, tc * 512:(tc + 1) * 512], PS[pslot], AF.Sigmoid), inc="SS")
                    evd[idx * 4 + tc] = ("SS", evs)
                    continue
                wt(C, "vector", "TS", tmm)
                if scale is None:
                    evs = op(C, "vector", lambda v, ss=ss, tc=tc, pslot=pslot: v.tensor_copy(
                        STG[ss][:, tc * 512:(tc + 1) * 512], PS[pslot]), inc="VS")
                else:
                    evs = op(C, "vector", lambda v, ss=ss, tc=tc, pslot=pslot, scale=scale: v.tensor_scalar(
                        STG[ss][:, tc * 512:(tc + 1) * 512], PS[pslot], scale, None, ALU.mult, ALU.bypass), inc="VS")
                evd[idx * 4 + tc] = ("VS", evs)
            wt(C, "sync", "SS" if scale == "sig" else "VS", evs)
            stg_free[ss] = dma(C, "sync", dr[ddst][h], STG[ss])
    for kind, wsrc, ddst, width, scale, sig in (
            (0, "L1_attn_Wk", "gla_kn", 1024, RSQ128, False),
            (1, "L1_attn_Wv", "gla_v", 2048, None, False)):
        ldr = None
        ev_eng = "SS" if sig else "VS"
        ev_ix = {}
        wt(C, "gpsimd", "TS", C.semn["TS"])
        for ci in range(NCI):
            ldr = dma(C, "gpsimd", WRR[:, ci * width:(ci + 1) * width],
                      dr[wsrc][ci * 128:(ci + 1) * 128, :])
        gi = 0
        for tt in range(NTT):
            ss = tt % 2
            if stg_free[ss] is not None:
                wt(C, ev_eng.replace("SS", "scalar").replace("VS", "vector"), "LD", stg_free[ss])
            evs = None
            for nch in range(width // 512):
                pslot = nch % 2 if width > 512 else gi % 2
                wt(C, "tensor", "GD", ldr)
                pv = ev_ix.get(gi - 2)
                if pv:
                    wt(C, "tensor", ev_eng, pv)
                elif tt == 0 and nch == 0:
                    wt(C, "tensor", "VS", C.semn["VS"])
                    wt(C, "tensor", "SS", C.semn["SS"])
                def rmm(t, tt=tt, nch=nch, pslot=pslot, width=width):
                    for ci in range(NCI):
                        ins = t.matmul(PS[pslot], XNT[:, ci * 2048 + tt * 128: ci * 2048 + tt * 128 + 128],
                                       WRR[:, ci * width + nch * 512: ci * width + (nch + 1) * 512],
                                       start=(ci == 0), stop=(ci == NCI - 1))
                    return ins
                tmm = op(C, "tensor", rmm, inc="TS")
                if sig:
                    wt(C, "scalar", "TS", tmm)
                    evs = op(C, "scalar", lambda s, ss=ss, nch=nch, pslot=pslot: s.activation(
                        STG[ss][:, nch * 512:(nch + 1) * 512], PS[pslot], AF.Sigmoid), inc="SS")
                else:
                    wt(C, "vector", "TS", tmm)
                    if scale is None:
                        evs = op(C, "vector", lambda v, ss=ss, nch=nch, pslot=pslot: v.tensor_copy(
                            STG[ss][:, nch * 512:(nch + 1) * 512], PS[pslot]), inc="VS")
                    else:
                        evs = op(C, "vector", lambda v, ss=ss, nch=nch, pslot=pslot, scale=scale: v.tensor_scalar(
                            STG[ss][:, nch * 512:(nch + 1) * 512], PS[pslot], scale, None, ALU.mult, ALU.bypass),
                            inc="VS")
                ev_ix[gi] = evs
                gi += 1
            wt(C, "sync", ev_eng, evs)
            stg_free[ss] = dma(C, "sync", dr[ddst][tt][:, 0:width], STG[ss][:, 0:width])
    p1_stores = C.semn["LD"]
    wt(C, "gpsimd", "TS", C.semn["TS"])
    ldwo = None
    for kh in range(16):
        ldwo = dma(C, "gpsimd", WOR[:, kh * 1024:(kh + 1) * 1024],
                   dr["L1_attn_Wo"][kh * 128:(kh + 1) * 128, :])
    smz = op(C, "vector", lambda v: v.memset(S, 0.0), inc="VS")
    if DEBUG_GLA:
        wt(C, "sync", "VS", C.semn["VS"])
        wt(C, "sync", "SS", C.semn["SS"])
        dma(C, "sync", dr["gla_gdbg"], tl["FPOOL"])
    barrier(C)

    p3_mm_last = None
    for sl in range(4):
        if sl == 0:
            wt(C, "sync", "LD", p1_stores)
        else:
            wt(C, "sync", "TS", C.semn["TS"])
            wt(C, "sync", "VS", C.semn["VS"])
        lq = None
        for h in range(8):
            lq = dma(C, "sync", QT[:, h * 512:(h + 1) * 512], dr["gla_q"][h][:, sl * 512:(sl + 1) * 512])
        for h in range(8):
            lq = dma(C, "sync", KT[:, h * 512:(h + 1) * 512], dr["gla_k"][h][:, sl * 512:(sl + 1) * 512])
        for ttr in range(4):
            lq = dma(C, "sync", KN[:, ttr * 1024:(ttr + 1) * 1024], dr["gla_kn"][sl * 4 + ttr])
        for ttr in range(4):
            lq = dma(C, "sync", VN[:, ttr * 2048:(ttr + 1) * 2048], dr["gla_v"][sl * 4 + ttr])
        for j in range(16):
            lq = dma(C, "sync", OG[:, j * 512:(j + 1) * 512],
                     dr["gla_ogT"][j][:, sl * 512:(sl + 1) * 512])
        loads_done = lq
        wt(C, "tensor", "LD", loads_done)
        if sl == 0:
            wt(C, "tensor", "GD", ldwo)
            wt(C, "tensor", "VS", smz)
        lb_free = [None, None]
        sc_free = [None, None]
        out_free = [None, None]
        st_free = [None, None]
        for nr in range(4):
            n = sl * 4 + nr
            for h in range(8):
                b = h % 2
                if lb_free[0] is not None:
                    wt(C, "tensor", "SS", lb_free[0])
                t1 = op(C, "tensor", lambda t, b=b, n=n, h=h: t.matmul(
                    PS[3][:, 0:128], HONES[:, h * 128:(h + 1) * 128], LT[:, n * 128:(n + 1) * 128],
                    start=True, stop=True), inc="TS")
                wt(C, "vector", "TS", t1)
                op(C, "vector", lambda v, b=b: v.tensor_copy(TBS[b], PS[3][:, 127:128]))
                v1 = op(C, "vector", lambda v, b=b, n=n, h=h: v.tensor_scalar(
                    NEGL[b], LNAT[:, n * 8 + h: n * 8 + h + 1], -1.0, None, ALU.mult, ALU.bypass), inc="VS")
                selfwait(C, "vector", "VS")
                if lb_free[0] is not None:
                    wt(C, "vector", "SS", lb_free[0])
                v1 = op(C, "vector", lambda v, b=b: v.tensor_scalar(
                    DTMP, PS[3][:, 0:128], NEGL[b], 0.0, ALU.add, ALU.min), inc="VS")
                wt(C, "scalar", "VS", v1)
                op(C, "scalar", lambda s, b=b: s.activation(GT[b], PS[3][:, 0:128], AF.Exp))
                op(C, "scalar", lambda s, b=b: s.activation(DT[b], DTMP, AF.Exp))
                op(C, "scalar", lambda s, b=b, n=n, h=h: s.activation(
                    KDC[b], LNAT[:, n * 8 + h: n * 8 + h + 1], AF.Exp, scale=-1.0, bias=TBS[b]))
                s1 = op(C, "scalar", lambda s, b=b: s.activation(GAM[b], TBS[b], AF.Exp), inc="SS")
                lb_free[0] = s1
                wt(C, "vector", "SS", s1)
                op(C, "vector", lambda v, b=b, h=h, nr=nr: v.tensor_tensor(
                    QG[b], QT[:, h * 512 + nr * 128: h * 512 + nr * 128 + 128], GT[b], ALU.mult))
                op(C, "vector", lambda v, b=b, h=h, nr=nr: v.tensor_scalar(
                    KD[b], KN[:, nr * 1024 + h * 128: nr * 1024 + h * 128 + 128], KDC[b], None, ALU.mult, ALU.bypass))
                v2 = op(C, "vector", lambda v, b=b, h=h: v.tensor_scalar(
                    SST[b], S[:, h * 256:(h + 1) * 256], GAM[b], None, ALU.mult, ALU.bypass), inc="VS")
                if sc_free[0] is not None:
                    wt(C, "tensor", "VS", sc_free[0])
                t2 = op(C, "tensor", lambda t, b=b, h=h, nr=nr: t.matmul(
                    PS[4][:, 0:128], KT[:, h * 512 + nr * 128: h * 512 + nr * 128 + 128],
                    QT[:, h * 512 + nr * 128: h * 512 + nr * 128 + 128], start=True, stop=True), inc="TS")
                wt(C, "vector", "TS", t2)
                op(C, "vector", lambda v, b=b: v.tensor_tensor(A1T[b], PS[4][:, 0:128], DT[b], ALU.mult))
                v3 = op(C, "vector", lambda v, b=b: v.tensor_tensor(AT[b], A1T[b], CM01, ALU.mult), inc="VS")
                sc_free[0] = v3
                wt(C, "tensor", "VS", v3)
                if out_free[b] is not None:
                    wt(C, "tensor", "VS", out_free[b])
                def t3f(t, b=b, h=h, nr=nr):
                    for dvt in range(2):
                        bank = (0 if b == 0 else 5) + dvt
                        t.matmul(PS[bank][:, 0:128],
                                 VN[:, nr * 2048 + h * 256 + dvt * 128: nr * 2048 + h * 256 + (dvt + 1) * 128],
                                 AT[b], start=True, stop=False)
                        ins = t.matmul(PS[bank][:, 0:128],
                                       S[:, h * 256 + dvt * 128: h * 256 + (dvt + 1) * 128],
                                       QG[b], start=False, stop=True)
                    return ins
                t3 = op(C, "tensor", t3f, inc="TS")
                if st_free[b] is not None:
                    wt(C, "tensor", "VS", st_free[b])
                def t4f(t, b=b, h=h, nr=nr):
                    t.matmul(PS[2 if b == 0 else 7][:, 0:256], KD[b],
                             VN[:, nr * 2048 + h * 256: nr * 2048 + (h + 1) * 256], start=True, stop=False)
                    return t.matmul(PS[2 if b == 0 else 7][:, 0:256], IDENT_BF, SST[b], start=False, stop=True)
                t4 = op(C, "tensor", t4f, inc="TS")
                wt(C, "vector", "TS", t4)
                st_free[b] = op(C, "vector", lambda v, b=b, h=h: v.tensor_copy(
                    S[:, h * 256:(h + 1) * 256], PS[2 if b == 0 else 7][:, 0:256]), inc="VS")
                for dvt in range(2):
                    of = op(C, "vector", lambda v, b=b, h=h, nr=nr, dvt=dvt: v.tensor_tensor(
                        OUT[:, (2 * h + dvt) * 512 + nr * 128: (2 * h + dvt) * 512 + (nr + 1) * 128],
                        PS[1 if b == 0 else 6][:, dvt * 128:(dvt + 1) * 128],
                        OG[:, (2 * h + dvt) * 512 + nr * 128: (2 * h + dvt) * 512 + (nr + 1) * 128],
                        ALU.mult), inc="VS")
                out_free[b] = of
        out_done = C.semn["VS"]
        if DEBUG_GLA:
            wt(C, "sync", "VS", out_done)
            for j in range(16):
                dma(C, "sync", dr["gla_outdbg"][j][:, sl * 512:(sl + 1) * 512], OUT[:, j * 512:(j + 1) * 512])
        evp3 = {}
        for ttr in range(4):
            for nch in range(2):
                i = ttr * 2 + nch
                pslot = i % 2
                wt(C, "tensor", "VS", out_done if i < 2 else evp3[i - 2])
                def ymm(t, ttr=ttr, nch=nch, pslot=pslot):
                    for kh in range(16):
                        ins = t.matmul(PS[pslot], OUT[:, kh * 512 + ttr * 128: kh * 512 + ttr * 128 + 128],
                                       WOR[:, kh * 1024 + nch * 512: kh * 1024 + (nch + 1) * 512],
                                       start=(kh == 0), stop=(kh == 15))
                    return ins
                tmm = op(C, "tensor", ymm, inc="TS")
                p3_mm_last = tmm
                wt(C, "vector", "TS", tmm)
                evp3[i] = op(C, "vector", lambda v, sl=sl, ttr=ttr, nch=nch, pslot=pslot: v.tensor_tensor(
                    X[:, (sl * 4 + ttr) * 1024 + nch * 512: (sl * 4 + ttr) * 1024 + (nch + 1) * 512],
                    PS[pslot],
                    X[:, (sl * 4 + ttr) * 1024 + nch * 512: (sl * 4 + ttr) * 1024 + (nch + 1) * 512],
                    ALU.add), inc="VS")
        wt(C, "vector", "TS", p3_mm_last)
    barrier(C)



def emit_glu(C, first):
    tl, dr, PS = C.tl, C.dr, C.ps
    emit_ln(C)
    barrier(C)
    X, XNAT, XNT = tl["X"], tl["XNAT"], tl["XNT"]
    PB = tl["POOL"]
    if first:
        CASTB = [PB[:, 0:1024], PB[:, 1024:2048]]
        ASM = [PB[:, 2048:3072], PB[:, 3072:4096]]
        PSB = [C.psb[6], C.psb[7]]
        cast_free = [None, None]
        asm_free = [None, None]
        ev_prep = {}
        pidx = 0
        for wname, dname in (("L1_ffn_w_gate", "wgt_t"), ("L1_ffn_w_up", "wut_t")):
            for m in range(24):
                cs = m % 2
                if cast_free[cs] is not None:
                    wt(C, "gpsimd", "TS", cast_free[cs])
                ldc = dma(C, "gpsimd", CASTB[cs], dr[wname][m * 128:(m + 1) * 128, :])
                if asm_free[cs] is not None:
                    wt(C, "vector", "LD", asm_free[cs])
                for ci in range(NCI):
                    slot = pidx % 2
                    wt(C, "tensor", "GD", ldc)
                    if pidx >= 2:
                        wt(C, "tensor", "VS", ev_prep[pidx - 2])
                    tm = op(C, "tensor", lambda t, cs=cs, ci=ci, slot=slot: t.transpose(
                        PSB[slot][:, 0:128], CASTB[cs][:, ci * 128:(ci + 1) * 128], tl["IDENT_BF"]), inc="TS")
                    if ci == NCI - 1:
                        cast_free[cs] = tm
                    wt(C, "vector", "TS", tm)
                    ev_prep[pidx] = op(C, "vector", lambda v, cs=cs, ci=ci, slot=slot: v.tensor_copy(
                        ASM[cs][:, ci * 128:(ci + 1) * 128], PSB[slot][:, 0:128]), inc="VS")
                    pidx += 1
                wt(C, "sync", "VS", ev_prep[pidx - 1])
                asm_free[cs] = dma(C, "sync", dr[dname][m], ASM[cs])
        WDC = PB[:, 4096:28672]
        ldc = None
        for mc in range(NCI):
            ldc = dma(C, "gpsimd", WDC[:, mc * 3072:(mc + 1) * 3072],
                      dr["L1_ffn_w_down"][mc * 128:(mc + 1) * 128, :])
        for kh in range(24):
            cs = kh % 2
            if asm_free[cs] is not None:
                wt(C, "vector", "LD", asm_free[cs])
            for mc in range(NCI):
                slot = pidx % 2
                wt(C, "tensor", "GD", ldc)
                if pidx >= 2:
                    wt(C, "tensor", "VS", ev_prep[pidx - 2])
                tm = op(C, "tensor", lambda t, cs=cs, mc=mc, kh=kh, slot=slot: t.transpose(
                    PSB[slot][:, 0:128], WDC[:, mc * 3072 + kh * 128: mc * 3072 + (kh + 1) * 128],
                    tl["IDENT_BF"]), inc="TS")
                wt(C, "vector", "TS", tm)
                ev_prep[pidx] = op(C, "vector", lambda v, cs=cs, mc=mc, slot=slot: v.tensor_copy(
                    ASM[cs][:, mc * 128:(mc + 1) * 128], PSB[slot][:, 0:128]), inc="VS")
                pidx += 1
            wt(C, "sync", "VS", ev_prep[pidx - 1])
            asm_free[cs] = dma(C, "sync", dr["wdt_t"][kh], ASM[cs])
        barrier(C)
    H = PB[:, 0:12288]
    WDA = XNAT[:, 0:16384]
    WDB = PB[:, 12288:20480]
    WG = [PB[:, 20480:21504], PB[:, 21504:22528]]
    WU = [PB[:, 22528:23552], PB[:, 23552:24576]]
    GS2 = [PB[:, 24576:25088], PB[:, 25088:25600]]

    def wdsl(kh):
        return WDA[:, kh * 1024:(kh + 1) * 1024] if kh < 16 else WDB[:, (kh - 16) * 1024:(kh - 15) * 1024]

    ldwd = None
    for kh in range(24):
        ldwd = dma(C, "sync", wdsl(kh), dr["wdt_t"][kh])
    wgl = [None, None]
    gate_free = [None, None]
    up_free = [None, None]
    h_written = None
    down_free = [None, None]
    for th in range(4):
        for m in range(24):
            ws = m % 2
            if wgl[ws] is not None:
                wt(C, "sync", "TS", wgl[ws][2])
            lg = dma(C, "sync", WG[ws], dr["wgt_t"][m])
            lu = dma(C, "sync", WU[ws], dr["wut_t"][m])
            wt(C, "tensor", "LD", lu)
            if gate_free[ws] is not None:
                wt(C, "tensor", "SS", gate_free[ws])
            if up_free[ws] is not None:
                wt(C, "tensor", "VS", up_free[ws])
            def gmm(t, ws=ws, th=th):
                for ci in range(NCI):
                    t.matmul(PS[0 + ws], WG[ws][:, ci * 128:(ci + 1) * 128],
                             XNT[:, ci * 2048 + th * 512: ci * 2048 + (th + 1) * 512],
                             start=(ci == 0), stop=(ci == NCI - 1))
                for ci in range(NCI):
                    ins = t.matmul(PS[2 + ws], WU[ws][:, ci * 128:(ci + 1) * 128],
                                   XNT[:, ci * 2048 + th * 512: ci * 2048 + (th + 1) * 512],
                                   start=(ci == 0), stop=(ci == NCI - 1))
                return ins
            tmm = op(C, "tensor", gmm, inc="TS")
            wgl[ws] = (lg, lu, tmm)
            wt(C, "scalar", "TS", tmm)
            sg = op(C, "scalar", lambda s, ws=ws: s.activation(GS2[ws], PS[0 + ws], AF.Silu, scale=1.0 / 127.0),
                    inc="SS")
            gate_free[ws] = sg
            wt(C, "vector", "SS", sg)
            h_written = op(C, "vector", lambda v, ws=ws, m=m: v.scalar_tensor_tensor(
                H[:, m * 512:(m + 1) * 512], PS[2 + ws], 1.0 / 127.0, GS2[ws], ALU.mult, ALU.mult), inc="VS")
            up_free[ws] = h_written
        barrier(C)
        evd = {}
        for ttr in range(4):
            for nch in range(2):
                i = ttr * 2 + nch
                pslot = i % 2
                wt(C, "tensor", "VS", h_written)
                if i == 0:
                    wt(C, "tensor", "LD", ldwd)
                if i >= 2:
                    wt(C, "tensor", "VS", evd[i - 2])
                def dmm(t, ttr=ttr, nch=nch, pslot=pslot):
                    for kh in range(24):
                        ins = t.matmul(PS[4 + pslot], H[:, kh * 512 + ttr * 128: kh * 512 + ttr * 128 + 128],
                                       wdsl(kh)[:, nch * 512:(nch + 1) * 512],
                                       start=(kh == 0), stop=(kh == 23))
                    return ins
                tmm = op(C, "tensor", dmm, inc="TS")
                wt(C, "vector", "TS", tmm)
                evd[i] = op(C, "vector", lambda v, th=th, ttr=ttr, nch=nch, pslot=pslot: v.scalar_tensor_tensor(
                    X[:, (th * 4 + ttr) * 1024 + nch * 512: (th * 4 + ttr) * 1024 + (nch + 1) * 512],
                    PS[4 + pslot], 0.5 / 127.0,
                    X[:, (th * 4 + ttr) * 1024 + nch * 512: (th * 4 + ttr) * 1024 + (nch + 1) * 512],
                    ALU.mult, ALU.add), inc="VS")
        barrier(C)
    barrier(C)



def build_program(groups):
    nc = bass.Bass("TRN2", target_bir_lowering=False)
    stack = ExitStack()
    dr = {}
    innames = set()
    xin = nc.dram_tensor("xin", [T, D], F32, kind="ExternalInput")
    xout = nc.dram_tensor("xout", [T, D], F32, kind="ExternalOutput")
    dr["xin"] = xin.ap()
    dr["xout"] = xout.ap()
    innames.add("xin")

    need = set(groups)

    def din(name, shape, dt):
        dr[name] = nc.dram_tensor(name, list(shape), dt, kind="ExternalInput").ap()
        innames.add(name)

    din("IDENT_BF_D", (128, 128), BF16)
    din("IDENT_F_D", (32, 32), F32)
    din("TRI_BF_D", (128, 128), BF16)
    din("CM01_BF_D", (128, 128), BF16)
    din("HEADONES_F_D", (8, 1024), F32)
    if need & {"moe", "moe2"}:
        din("L0_ffn_Wr", (D, NE), F32)
        din("L0_ffn_br", (NE,), F32)
        din("L0_ffn_W1", (NE, D, 4096), F32)
        din("L0_ffn_b1", (NE, 4096), F32)
        din("L0_ffn_W2", (NE, 4096, D), F32)
        din("L0_ffn_b2", (NE, D), F32)
        mk = "ExternalOutput" if DEBUG_MOE else "Internal"
        dr["moe_buf"] = nc.dram_tensor("moe_buf", [NE * CAP, D], BF16, kind=mk).ap()
        dr["moe_y"] = nc.dram_tensor("moe_y", [NE * CAP, D], BF16, kind=mk).ap()
        if DEBUG_MOE:
            dr["moe_fdbg"] = nc.dram_tensor("moe_fdbg", [128, 4448], F32, kind="ExternalOutput").ap()
            dr["moe_sdbg"] = nc.dram_tensor("moe_sdbg", [128, 32], U32, kind="ExternalOutput").ap()
            dr["moe_hdbg"] = nc.dram_tensor("moe_hdbg", [128, 10240], BF16, kind="ExternalOutput").ap()
            dr["moe_bdbg"] = nc.dram_tensor("moe_bdbg", [128, 2560], BF16, kind="ExternalOutput").ap()
    if "attn" in need:
        din("L0_attn_Wqkv", (D, 3 * D), F32)
        din("L0_attn_bqkv", (3 * D,), F32)
        din("L0_attn_Wout", (D, D), F32)
        din("L0_attn_bout", (D,), F32)
    if "gla" in need:
        din("L1_attn_Wq", (D, D), F32)
        din("L1_attn_Wk", (D, D), F32)
        din("L1_attn_Wv", (D, 2 * D), F32)
        din("L1_attn_Wo", (2 * D, D), F32)
        din("L1_attn_Wg", (D, 8), F32)
        din("L1_attn_bg", (8,), F32)
        din("L1_attn_Wog", (D, 2 * D), F32)
        gk = "ExternalOutput" if DEBUG_GLA else "Internal"
        if DEBUG_GLA:
            dr["gla_outdbg"] = nc.dram_tensor("gla_outdbg", [16, 128, T], BF16, kind="ExternalOutput").ap()
            dr["gla_gdbg"] = nc.dram_tensor("gla_gdbg", [128, 4448], F32, kind="ExternalOutput").ap()
        dr["gla_q"] = nc.dram_tensor("gla_q", [8, 128, T], BF16, kind=gk).ap()
        dr["gla_k"] = nc.dram_tensor("gla_k", [8, 128, T], BF16, kind=gk).ap()
        dr["gla_kn"] = nc.dram_tensor("gla_kn", [16, 128, D], BF16, kind=gk).ap()
        dr["gla_v"] = nc.dram_tensor("gla_v", [16, 128, 2 * D], BF16, kind=gk).ap()
        dr["gla_ogT"] = nc.dram_tensor("gla_ogT", [16, 128, T], BF16, kind=gk).ap()
    if need & {"glu", "glu2"}:
        din("L1_ffn_w_gate", (3072, D), I32)
        din("L1_ffn_w_up", (3072, D), I32)
        din("L1_ffn_w_down", (D, 3072), I32)
        dr["wgt_t"] = nc.dram_tensor("wgt_t", [24, 128, 1024], BF16, kind="Internal").ap()
        dr["wut_t"] = nc.dram_tensor("wut_t", [24, 128, 1024], BF16, kind="Internal").ap()
        dr["wdt_t"] = nc.dram_tensor("wdt_t", [24, 128, 1024], BF16, kind="Internal").ap()

    plan = Plan()
    C = Ctx()
    C.nc = nc
    C.plan = plan
    C.dr = dr
    C.semh = {}
    C.semn = {}
    for name in ("LD", "GD", "SD", "TS", "VS", "SS", "GS", "BAR"):
        C.semh[name] = stack.enter_context(nc.semaphore(name))
        C.semn[name] = 0

    tl = {}

    def sb(name, shape, dt):
        tl[name] = stack.enter_context(nc.sbuf_tensor(name, list(shape), dt))[:]

    sb("X", (128, 16384), F32)
    sb("XNAT", (128, 16384), BF16)
    sb("XNT", (128, 16384), BF16)
    sb("POOL", (128, 28800), BF16)
    sb("FPOOL", (128, 4448), F32)
    tl["STATS"] = tl["FPOOL"][:, 0:192]
    tl["MV"] = tl["FPOOL"][:, 192:224]
    tl["LNV"] = tl["FPOOL"][:, 224:240]
    tl["RSTD"] = tl["FPOOL"][:, 240:256]
    sb("COL8", (8, 1), F32)
    sb("B1ST", (32, 128), F32)
    sb("SLOTU", (128, 32), U32)
    sb("I8U", (128, 128), mybir.dt.uint16)
    sb("IDENT_BF", (128, 128), BF16)
    sb("IDENT_F", (32, 32), F32)
    sb("TRI_BF", (128, 128), BF16)
    sb("CM01_BF", (128, 128), BF16)
    sb("HEADONES_F", (8, 1024), F32)
    sb("ONESCOL_BF", (128, 1), BF16)
    sb("ONES1x64_BF", (1, 64), BF16)
    C.tl = tl
    psh = [stack.enter_context(nc.psum_tensor(f"PS{i}", [128, 512], F32)) for i in range(8)]
    C.ps = [p[:] for p in psh]
    C.psb = [p.bitcast(BF16)[:, 0:128] for p in psh]

    tld = dma(C, "sync", tl["X"].rearrange("p (t c) -> p t c", t=16),
              dr["xin"].rearrange("(t p) c -> p t c", p=128))
    dma(C, "sync", tl["IDENT_BF"], dr["IDENT_BF_D"])
    dma(C, "sync", tl["IDENT_F"], dr["IDENT_F_D"])
    dma(C, "sync", tl["TRI_BF"], dr["TRI_BF_D"])
    dma(C, "sync", tl["CM01_BF"], dr["CM01_BF_D"])
    tcd = dma(C, "sync", tl["HEADONES_F"], dr["HEADONES_F_D"])
    op(C, "vector", lambda v: v.memset(tl["ONESCOL_BF"], 1.0))
    op(C, "vector", lambda v: v.memset(tl["FPOOL"][:, 4440:4441], 1e-5))
    op(C, "vector", lambda v: v.memset(tl["FPOOL"][:, 4441:4442], -10.0))
    op(C, "vector", lambda v: v.memset(tl["ONES1x64_BF"], 1.0), inc="VS")
    def _reginit(g):
        C.rfree = g.alloc_register("rfree")
        g.free_register(C.rfree)
    C.plan.on("gpsimd", _reginit)
    for eng in ENGINES:
        wt(C, eng, "LD", tcd)
    barrier(C)

    seen = set()
    for g in groups:
        if g in ("moe", "moe2"):
            emit_moe(C, first=not (seen & {"moe", "moe2"}))
        elif g == "attn":
            emit_attn(C)
        elif g == "gla":
            emit_gla(C)
        elif g in ("glu", "glu2"):
            emit_glu(C, first=not (seen & {"glu", "glu2"}))
        seen.add(g)

    st = dma(C, "sync", dr["xout"].rearrange("(t p) c -> p t c", p=128),
             tl["X"].rearrange("p (t c) -> p t c", t=16))
    wt(C, "sync", "LD", st)

    with nc.Block() as block:
        @block.sync
        def _(e):
            for f in plan.q["sync"]:
                f(e)

        @block.scalar
        def _(e):
            for f in plan.q["scalar"]:
                f(e)

        @block.vector
        def _(e):
            for f in plan.q["vector"]:
                f(e)

        @block.tensor
        def _(e):
            for f in plan.q["tensor"]:
                f(e)

        @block.gpsimd
        def _(e):
            for f in plan.q["gpsimd"]:
                f(e)

    return nc, stack, innames



def make_consts():
    c = {}
    c["IDENT_BF_D"] = np.eye(128, dtype=ml_dtypes.bfloat16)
    c["IDENT_F_D"] = np.eye(32, dtype=np.float32)
    c["TRI_BF_D"] = np.triu(np.ones((128, 128), dtype=np.float32), 1).astype(ml_dtypes.bfloat16)
    c["CM01_BF_D"] = np.triu(np.ones((128, 128), dtype=np.float32), 0).astype(ml_dtypes.bfloat16)
    ho = np.zeros((8, 1024), dtype=np.float32)
    for h in range(8):
        ho[h, h * 128:(h + 1) * 128] = 1.0
    c["HEADONES_F_D"] = ho
    return c


WNAMES = ["L0_attn_Wqkv", "L0_attn_bqkv", "L0_attn_Wout", "L0_attn_bout",
          "L0_ffn_Wr", "L0_ffn_br", "L0_ffn_W1", "L0_ffn_b1", "L0_ffn_W2", "L0_ffn_b2",
          "L1_attn_Wq", "L1_attn_Wk", "L1_attn_Wv", "L1_attn_Wo", "L1_attn_Wg",
          "L1_attn_bg", "L1_attn_Wog",
          "L1_ffn_w_gate", "L1_ffn_w_up", "L1_ffn_w_down"]


def kernel(**inputs):
    ins = {k: np.ascontiguousarray(np.asarray(v)) for k, v in inputs.items()}
    consts = make_consts()
    x = ins["L0_attn_x"]
    xcur = [np.ascontiguousarray(x[b]) for b in range(NCORES)]
    for groups in LAUNCH_GROUPS:
        nc, stack, innames = build_program(groups)
        in_maps = []
        for b in range(NCORES):
            m = {"xin": xcur[b]}
            for k, v in consts.items():
                if k in innames:
                    m[k] = v
            for k in WNAMES:
                if k in innames:
                    m[k] = ins[k]
            in_maps.append(m)
        res = run_bass_kernel_spmd(nc, in_maps, core_ids=list(range(NCORES)))
        stack.close()
        xcur = [np.asarray(res.results[b]["xout"]) for b in range(NCORES)]
    return np.stack(xcur, axis=0)
